# version 02
# baseline (speedup 1.0000x reference)
import math
import numpy as np
import ml_dtypes
import concourse.bass as bass
import concourse.mybir as mybir
from concourse.bass_utils import run_bass_kernel_spmd

F32 = mybir.dt.float32
BF16 = mybir.dt.bfloat16
AF = mybir.ActivationFunctionType
OP = mybir.AluOpType
AX = mybir.AxisListType

B_FULL, T, D = 32, 2048, 1024
NCORES = 8
BPC = B_FULL // NCORES
DILS = [1, 2, 4, 8]
LS = [16, 8, 4, 2]
NORM = 1.0 / math.sqrt(128.0)
NEG = -1.0e9
EPS = 1e-5

JT_FFN = [
    [(0, 1, 2, 512)],
    [(0, 1025, 2, 512)],
    [(0, 2, 4, 512)],
    [(0, 4, 8, 256), (256, 0, 8, 256)],
]
JT_OP = [
    [(0, 0, 1, 2, 512)],
    [(0, 0, 1025, 2, 512)],
    [(0, 0, 2, 4, 512), (1, 0, 1, 2, 512)],
    [(0, 0, 4, 8, 256), (0, 256, 0, 8, 256),
     (1, 0, 2, 4, 256), (1, 256, 0, 4, 256),
     (2, 0, 1, 2, 256), (2, 256, 0, 2, 256),
     (3, 256, 0, 1, 256)],
]

DEBUG_TAPS = []
DEBUG_BI0_ONLY = False
DEBUG_LN_ONLY = False


def _gperm():
    t = np.arange(T)
    return np.concatenate([t[t % 2 == 1], t[t % 4 == 2],
                           t[t % 8 == 4], t[t % 8 == 0]])


class _Op:
    __slots__ = ("eng", "waits", "fn", "needed", "val", "is_dma", "sem",
                 "lane", "raw_waits")

    def __init__(self, eng, waits, fn, is_dma, lane, raw_waits=()):
        self.eng, self.waits, self.fn = eng, waits, fn
        self.needed, self.val, self.is_dma = False, None, is_dma
        self.sem = None
        self.lane = lane
        self.raw_waits = raw_waits


class Builder:
    def __init__(self, nc):
        self.nc = nc
        self.ops = []

    def emit(self, eng, fn, waits=(), is_dma=False, lane=None,
             raw_waits=()):
        ws = tuple(w for w in waits if w is not None)
        for w in ws:
            assert isinstance(w, _Op)
        op = _Op(eng, ws, fn, is_dma, lane, raw_waits)
        self.ops.append(op)
        return op

    def dma(self, eng, out, in_, waits=(), lane="g"):
        return self.emit(eng, lambda e: e.dma_start(out=out, in_=in_),
                         waits, is_dma=True, lane=lane)


def _sl(ap, st0, stp, cnt):
    return ap[:, st0:st0 + stp * (cnt - 1) + 1:stp]


def build(dump=()):
    nc = bass.Bass("TRN2", target_bir_lowering=False, debug=False)
    b = Builder(nc)

    def din(name, shape, dt):
        return nc.dram_tensor(name, list(shape), dt, kind="ExternalInput").ap()

    x_d = din("x", (BPC, D, T), F32)
    xg_d = din("xg", (BPC, D, T), F32)
    wqt_d = din("wqt", (4, D, 128), F32)
    wkt_d = din("wkt", (4, D, 128), F32)
    wvt_d = din("wvt", (4, D, 128), F32)
    wot_d = din("wot", (512, D), F32)
    bo_d = din("bo_t", (128, 8), F32)
    fb_d = din("fb_t", (128, 8), F32)
    wrt_d = din("wrt", (D, D), F32)
    wit_d = din("wit", (D, D), F32)
    c_d = din("cdft", (D, D), BF16)
    s_d = din("sdft", (D, D), BF16)
    sn_d = din("sndft", (D, D), BF16)
    mb_d = din("mb", (4, 128, 128), F32)
    id_d = din("ident", (128, 128), F32)
    csl_d = din("csl", (D, 128), BF16)
    snsl_d = din("snsl", (D, 128), BF16)
    ssl_d = din("ssl", (D, 128), BF16)
    out_d = nc.dram_tensor("out", [BPC, D, T], F32, kind="ExternalOutput").ap()
    msl_d = nc.dram_tensor("m_slice", [D, 128], BF16, kind="Internal").ap()
    mfl_d = nc.dram_tensor("m_full", [8 * D, 128], BF16, kind="Internal").ap()

    def dbg_out(name, shape, dt=F32):
        return nc.dram_tensor("dbg_" + name, list(shape), dt,
                              kind="ExternalOutput").ap()

    import contextlib
    stack = contextlib.ExitStack()
    sb = lambda name, shape, dt: stack.enter_context(
        nc.sbuf_tensor(name, list(shape), dt))
    psum = lambda name: stack.enter_context(
        nc.psum_tensor(name, [128, 512], F32))

    with stack:
        rsem = stack.enter_context(nc.semaphore("rsem"))
        lsem = stack.enter_context(nc.semaphore("lsem"))
        xfp = [sb(f"xfp{i}", [128, 1024], F32) for i in range(2)]
        xb = [sb(f"xb{c}", [128, T], BF16) for c in range(8)]
        xn = [sb(f"xn{c}", [128, T], BF16) for c in range(8)]
        t1 = sb("t1", [128, T], BF16)
        mu_bf = sb("mu_bf", [128, T], BF16)
        invs_bf = sb("invs_bf", [128, T], BF16)
        xsq_t = [sb(f"xsq{i}", [128, 512], BF16) for i in range(4)]
        st_a = sb("st_a", [128, 512], F32)
        st_b = sb("st_b", [128, 512], F32)
        st_c = sb("st_c", [128, 512], F32)
        ones = sb("ones", [128, 128], BF16)
        czc = sb("czc", [128, 2], F32)
        wq_sb = [[sb(f"wq{h}_{c}", [128, 128], BF16) for c in range(8)]
                 for h in range(4)]
        wk_sb = [[sb(f"wk{h}_{c}", [128, 128], BF16) for c in range(8)]
                 for h in range(4)]
        wv_sb = [[sb(f"wv{h}_{c}", [128, 128], BF16) for c in range(8)]
                 for h in range(4)]
        wo_sb = [sb(f"wo{h}", [128, 1024], BF16) for h in range(4)]
        fb_sb = sb("fb_sb", [128, 8], F32)
        bo_sb = sb("bo_sb", [128, 8], F32)
        mb_sb = [sb(f"mb{h}", [128, 128], F32) for h in range(4)]
        m_all = sb("m_all", [128, 8192], BF16)
        m_sb = [m_all[:, c * 1024:(c + 1) * 1024] for c in range(8)]
        q_sb = sb("q_sb", [128, 2048], BF16)
        k_sb = sb("k_sb", [128, 2048], BF16)
        v_pm = sb("v_pm", [128, 2048], F32)
        v_tm = sb("v_tm", [128, 2048], BF16)
        ident = sb("ident_sb", [128, 128], F32)
        csl_sb = [sb(f"csl{e}", [128, 128], BF16) for e in range(8)]
        snsl_sb = [sb(f"snsl{e}", [128, 128], BF16) for e in range(8)]
        ssl_sb = [sb(f"ssl{e}", [128, 128], BF16) for e in range(8)]
        at_sb = [sb(f"at{h}", [128, 128 * LS[h]], BF16) for h in range(4)]
        t_sc = [sb(f"tsc{i}", [128, 128], F32) for i in range(2)]
        e_bf = [sb(f"ebf{i}", [128, 128], BF16) for i in range(2)]
        p_bf = [sb(f"pbf{i}", [128, 128], BF16) for i in range(2)]
        nrm = [sb(f"nrm{i}", [128, 1], F32) for i in range(2)]
        rs = [sb(f"rs{i}", [128, 1], F32) for i in range(2)]
        rsi = [sb(f"rsi{i}", [128, 1], F32) for i in range(2)]
        g_sb = [sb(f"g{i}", [128, 512], F32) for i in range(2)]
        og_sb = [sb(f"og{i}", [128, 512], F32) for i in range(2)]
        out_sb = [sb(f"os{i}", [128, 1024], F32) for i in range(2)]
        xg_sb = [sb(f"xgs{i}", [128, 1024], F32) for i in range(2)]
        ps = [psum(f"ps{i}") for i in range(8)]

        c_sb = [xb[ec][:, 0:1024] for ec in range(8)]
        sneg_sb = [xb[ec][:, 1024:2048] for ec in range(8)]
        s_sb = [xn[ec][:, 0:1024] for ec in range(8)]
        wrT_b = [xn[ec][:, 1024:2048] for ec in range(8)]
        wiT_b = [mu_bf[:, 0:1024], mu_bf[:, 1024:2048],
                 invs_bf[:, 0:1024], invs_bf[:, 1024:2048],
                 t1[:, 0:1024], t1[:, 1024:2048],
                 xg_sb[0][:, :].bitcast(BF16)[:, 0:1024],
                 xg_sb[0][:, :].bitcast(BF16)[:, 1024:2048]]
        br_sb = [at_sb[0][:, 0:1024], at_sb[0][:, 1024:2048],
                 v_tm[:, 0:1024], v_tm[:, 1024:2048],
                 q_sb[:, 0:1024], q_sb[:, 1024:2048],
                 k_sb[:, 0:1024], k_sb[:, 1024:2048]]
        _vpmb = v_pm[:, :].bitcast(BF16)
        brl_sb = [q_sb[:, e * 128:(e + 1) * 128] for e in range(8)]
        bil_sb = [k_sb[:, e * 128:(e + 1) * 128] for e in range(8)]
        msl_sb = [v_tm[:, e * 128:(e + 1) * 128] for e in range(8)]
        bi_sb = [at_sb[1][:, :], _vpmb[:, 0:1024], _vpmb[:, 1024:2048],
                 _vpmb[:, 2048:3072],
                 out_sb[0][:, :].bitcast(BF16)[:, 0:1024],
                 out_sb[0][:, :].bitcast(BF16)[:, 1024:2048],
                 out_sb[1][:, :].bitcast(BF16)[:, 0:1024],
                 out_sb[1][:, :].bitcast(BF16)[:, 1024:2048]]

        bank_w = [None] * 8
        bank_r = [None] * 8

        def mm_group(bank, mms, waits=()):
            ws0 = [w for w in waits if w is not None]
            if bank_r[bank] is not None:
                ws0.append(bank_r[bank])
            ev = None
            n = len(mms)
            for i, mm in enumerate(mms):
                w = list(ws0) if i == 0 else []
                if mm[0] == "T":
                    _, src, idt = mm
                    ev = b.emit("tensor",
                                lambda e, src=src, idt=idt, bank=bank:
                                e.transpose(ps[bank][:, 0:128], src, idt),
                                waits=tuple(w))
                    continue
                lh, rh, o = mm[0], mm[1], mm[2]
                if len(mm) > 3 and mm[3] is not None:
                    w.append(mm[3])
                ev = b.emit("tensor",
                            lambda e, lh=lh, rh=rh, o=o, s0=(i == 0),
                            s1=(i == n - 1):
                            e.matmul(o, lh, rh, start=s0, stop=s1),
                            waits=tuple(w))
            bank_w[bank] = ev
            return ev

        def reader(eng, fn, bank, waits=()):
            ev = b.emit(eng, fn, waits=tuple(waits) + (bank_w[bank],))
            bank_r[bank] = ev
            return ev

        ones_ev = b.emit("vector", lambda e: e.memset(ones[:, :], 1.0))
        b.emit("vector", lambda e: e.memset(czc[:, 0:1], 0.0))
        cz_ev = b.emit("vector", lambda e: e.memset(czc[:, 1:2], EPS))
        nc.const_aps.register(F32, 0.0, czc[:, 0:1])
        nc.const_aps.register(F32, EPS, czc[:, 1:2])
        last_const = None
        for ec in range(8):
            b.dma("gpsimd", c_sb[ec], c_d[ec * 128:(ec + 1) * 128, :], lane="gc")
            b.dma("gpsimd", s_sb[ec], s_d[ec * 128:(ec + 1) * 128, :], lane="gc")
            b.dma("gpsimd", csl_sb[ec][:, :], csl_d[ec * 128:(ec + 1) * 128, :],
                  lane="gc")
            b.dma("gpsimd", ssl_sb[ec][:, :], ssl_d[ec * 128:(ec + 1) * 128, :],
                  lane="gc")
            last_const = b.dma("gpsimd", snsl_sb[ec][:, :],
                               snsl_d[ec * 128:(ec + 1) * 128, :], lane="gc")
        xfp_last = [None, None]
        last_wcast = None
        for j, (src, dsts) in enumerate([(wrt_d, wrT_b), (wit_d, wiT_b)]):
            for ec in range(8):
                i = (j * 8 + ec) % 2
                ld = b.dma("sync", xfp[i][:, :],
                           src[ec * 128:(ec + 1) * 128, :],
                           waits=(xfp_last[i],), lane=f"xf{i}")
                cast = b.emit("vector",
                              lambda e, d=dsts[ec], s_=xfp[i]:
                              e.tensor_copy(d, s_[:, :]), waits=(ld,))
                xfp_last[i] = cast
                last_wcast = cast

        nbank = 0
        last_s1 = None
        for dst, lh_a, rh_a, lh_b, rh_b in (
                (brl_sb, wrT_b, csl_sb, wiT_b, snsl_sb),
                (bil_sb, wiT_b, csl_sb, wrT_b, ssl_sb)):
            for rc in range(8):
                bank = nbank % 4
                nbank += 1
                mms = [(lh_a[ec][:, rc * 128:(rc + 1) * 128],
                        rh_a[ec][:, :], ps[bank][:, 0:128])
                       for ec in range(8)]
                mms += [(lh_b[ec][:, rc * 128:(rc + 1) * 128],
                         rh_b[ec][:, :], ps[bank][:, 0:128])
                        for ec in range(8)]
                mm_group(bank, mms, waits=(last_const, last_wcast))
                last_s1 = reader(
                    "vector", lambda e, d=dst[rc], bank=bank:
                    e.tensor_copy(d[:, :], ps[bank][:, 0:128]), bank)
        m_mm_done = None
        msl_ev = None
        for dc in range(8):
            bank = nbank % 4
            nbank += 1
            mms = [(c_sb[ec][:, dc * 128:(dc + 1) * 128],
                    brl_sb[ec][:, :], ps[bank][:, 0:128]) for ec in range(8)]
            mms += [(s_sb[ec][:, dc * 128:(dc + 1) * 128],
                     bil_sb[ec][:, :], ps[bank][:, 0:128]) for ec in range(8)]
            m_mm_done = mm_group(bank, mms, waits=(last_s1,))
            msl_ev = reader("vector", lambda e, dc=dc, bank=bank:
                            e.tensor_copy(msl_sb[dc][:, :], ps[bank][:, 0:128]),
                            bank)
        msl_dma = None
        for dc in range(8):
            msl_dma = b.dma("gpsimd", msl_d[dc * 128:(dc + 1) * 128, :],
                            msl_sb[dc][:, :], waits=(msl_ev,), lane="gm")

        m_ready = None
        if False and "msb" in dump:
            mo = dbg_out("msb", (8, 128, 1024), BF16)
            for dc in range(8):
                b.dma("gpsimd", mo[dc], m_sb[dc][:, :], waits=(m_ready,), lane="gt")

        xfp_last = [m_mm_done if xfp_last[i] is None else xfp_last[i]
                    for i in range(2)]
        nw = 0
        w_ready = None
        for (src, dst_g, scale) in ((wqt_d, wq_sb, NORM), (wkt_d, wk_sb, 1.0),
                                    (wvt_d, wv_sb, 1.0)):
            for h in range(4):
                for dc in range(8):
                    i = nw % 2
                    nw += 1
                    ld = b.dma("sync", xfp[i][:, 0:128],
                               src[h, dc * 128:(dc + 1) * 128, :],
                               waits=(xfp_last[i],), lane=f"xf{i}")
                    if scale == 1.0:
                        cast = b.emit("vector",
                                      lambda e, d=dst_g[h][dc], s_=xfp[i]:
                                      e.tensor_copy(d[:, :], s_[:, 0:128]),
                                      waits=(ld,))
                    else:
                        cast = b.emit("scalar",
                                      lambda e, d=dst_g[h][dc], s_=xfp[i],
                                      sc=scale:
                                      e.activation(d[:, :], s_[:, 0:128],
                                                   AF.Copy, bias=0.0,
                                                   scale=sc),
                                      waits=(ld, cz_ev))
                    xfp_last[i] = cast
                    w_ready = cast
        for h in range(4):
            i = nw % 2
            nw += 1
            ld = b.dma("sync", xfp[i][:, :],
                       wot_d[h * 128:(h + 1) * 128, :], waits=(xfp_last[i],),
                       lane=f"xf{i}")
            cast = b.emit("vector", lambda e, d=wo_sb[h], s_=xfp[i]:
                          e.tensor_copy(d[:, :], s_[:, :]), waits=(ld,))
            xfp_last[i] = cast
            w_ready = cast
        b.dma("gpsimd", fb_sb[:, :], fb_d[:, :], lane="gs")
        b.dma("gpsimd", bo_sb[:, :], bo_d[:, :], lane="gs")
        b.dma("gpsimd", ident[:, :], id_d[:, :], lane="gs")
        small_ready = None
        for h in range(4):
            small_ready = b.dma("gpsimd", mb_sb[h][:, :], mb_d[h], lane="gs")

        xb_guard = [[m_mm_done] for _ in range(8)]
        xn_guard = [[m_mm_done] for _ in range(8)]
        lnbuf_guard = [m_mm_done]
        g_guard = [None, None]
        og_dma = [None, None]
        xg_guard = [[m_mm_done], [None]]
        tsc_guard = [None, None]
        ebf_guard = [None, None]
        pbf_guard = [None, None]
        sq_mm = [None] * 4
        last_stores = [None, None]

        nbi = 1 if DEBUG_BI0_ONLY else BPC
        for bi in range(nbi):
            cast_ev = [None] * 8
            for dc in range(8):
                ev = None
                for hf in range(2):
                    i = (dc * 2 + hf) % 2
                    w = [xfp_last[i]] + (xb_guard[dc] if hf == 0 else [])
                    ld = b.dma("sync", xfp[i][:, :],
                               x_d[bi, dc * 128:(dc + 1) * 128,
                                   hf * 1024:(hf + 1) * 1024],
                               waits=tuple(w), lane=f"xf{i}")
                    ev = b.emit("vector",
                                lambda e, dc=dc, hf=hf, s_=xfp[i]:
                                e.tensor_copy(
                                    xb[dc][:, hf * 1024:(hf + 1) * 1024],
                                    s_[:, :]), waits=(ld,))
                    xfp_last[i] = ev
                cast_ev[dc] = ev
            if "xb" in dump and bi == 0:
                xo2 = dbg_out("xb", (8, 128, T), BF16)
                for dc in range(8):
                    b.dma("gpsimd", xo2[dc], xb[dc][:, :],
                          waits=(cast_ev[dc],), lane="gt")
            sx_last = None
            stats_done = None
            prev_sstd = None
            for tt in range(4):
                sl = slice(tt * 512, (tt + 1) * 512)
                mms = [(ones[:, :], xb[dc][:, sl], ps[0][:, :],
                        cast_ev[dc] if tt == 0 else None) for dc in range(8)]
                sx_last = mm_group(0, mms, waits=(ones_ev,))
                ev = None
                for dc in range(8):
                    j = dc % 4
                    sq = b.emit("vector",
                                lambda e, j=j, dc=dc, sl=sl:
                                e.tensor_mul(xsq_t[j][:, :], xb[dc][:, sl],
                                             xb[dc][:, sl]),
                                waits=(cast_ev[dc], sq_mm[j]))
                    w = [bank_r[1], ones_ev] if dc == 0 else []
                    w.append(sq)
                    ev = b.emit("tensor",
                                lambda e, j=j, s0=(dc == 0), s1=(dc == 7):
                                e.matmul(ps[1][:, :], ones[:, :],
                                         xsq_t[j][:, :], start=s0, stop=s1),
                                waits=tuple(x for x in w if x is not None))
                    sq_mm[j] = ev
                bank_w[1] = ev
                mu = reader("vector", lambda e: e.tensor_scalar_mul(
                    st_a[:, :], ps[0][:, :], 1.0 / D), 0,
                    waits=(lnbuf_guard[0] if tt == 0 else None,))
                b.emit("vector", lambda e, sl=sl: e.tensor_copy(
                    mu_bf[:, sl], st_a[:, :]))
                musq = b.emit("vector", lambda e: e.tensor_mul(
                    st_b[:, :], st_a[:, :], st_a[:, :]))
                var = reader("vector", lambda e: e.scalar_tensor_tensor(
                    st_c[:, :], ps[1][:, :], 1.0 / D, st_b[:, :],
                    OP.mult, OP.subtract), 1, waits=(musq,))
                sstd = b.emit("scalar", lambda e: e.activation(
                    st_b[:, :], st_c[:, :], AF.Sqrt, bias=czc[:, 1:2],
                    scale=1.0), waits=(var, prev_sstd))
                prev_sstd = sstd
                rec = b.emit("vector", lambda e: e.reciprocal(
                    st_a[:, :], st_b[:, :]), waits=(sstd,))
                stats_done = b.emit("vector", lambda e, sl=sl: e.tensor_copy(
                    invs_bf[:, sl], st_a[:, :]))
            apply_ev = [None] * 8
            for dc in range(8):
                b.emit("vector", lambda e, dc=dc: e.tensor_sub(
                    t1[:, :], xb[dc][:, :], mu_bf[:, :]),
                    waits=(stats_done,) + tuple(xn_guard[dc]))
                apply_ev[dc] = b.emit("vector", lambda e, dc=dc: e.tensor_mul(
                    xn[dc][:, :], t1[:, :], invs_bf[:, :]))
            xn_ready = apply_ev[7]
            for dc in range(8):
                xb_guard[dc] = [apply_ev[dc], sx_last]
            if "stats" in dump and bi == 0:
                so = dbg_out("stats", (2, 128, T), BF16)
                b.dma("gpsimd", so[0], mu_bf[:, :], waits=(stats_done,), lane="gt")
                b.dma("gpsimd", so[1], invs_bf[:, :], waits=(stats_done,), lane="gt")
            if "xn" in dump and bi == 0:
                xo = dbg_out("xn", (8, 128, T), BF16)
                for dc in range(8):
                    b.dma("gpsimd", xo[dc], xn[dc][:, :], waits=(xn_ready,), lane="gt")

            if DEBUG_LN_ONLY:
                add = b.emit("vector", lambda e: e.tensor_copy(
                    out_sb[0][:, :], xg_sb[0][:, :]),
                    waits=(xn_ready, og_dma[0], xg_guard[0][0]))
                stv = b.dma("sync", out_d[bi, 0:128, 0:1024],
                            out_sb[0][:, :], waits=(add,), lane="st0")
                og_dma[0] = stv
                last_stores[0] = stv
                for dc in range(8):
                    xn_guard[dc] = [sx_last]
                lnbuf_guard = [apply_ev[7]]
                continue

            if bi == 0:
                cc_ev = b.emit("gpsimd",
                               lambda e: e.collective_compute(
                                   "AllGather", OP.bypass,
                                   replica_groups=[list(range(8))],
                                   ins=[msl_d[:, :].opt()],
                                   outs=[mfl_d[:, :].opt()]),
                               waits=(msl_dma, xn_ready))

            att_ev = [None] * 4
            for h in range(4):
                L, dil = LS[h], DILS[h]
                ncols = 128 * L
                qk_ev, v_ev = {}, {}
                for g in range((L + 3) // 4):
                    nb = min(4, L - g * 4)
                    for (wt, bank, dst, eng) in ((wq_sb, 2, q_sb, "scalar"),
                                                 (wk_sb, 3, k_sb, "scalar"),
                                                 (wv_sb, 4 + g % 2, v_pm,
                                                  "vector")):
                        mms = []
                        for dc in range(8):
                            for nn in range(nb):
                                n = g * 4 + nn
                                mms.append((
                                    wt[h][dc][:, :],
                                    xn[dc][:, n * dil * 128:
                                           n * dil * 128 + 128],
                                    ps[bank][:, nn * 128:nn * 128 + 128]))
                        mm_group(bank, mms, waits=(xn_ready, w_ready))
                        src = (ps[bank][:, 0:nb * 128]
                               .rearrange("p (n s) -> p n s", s=128))
                        dstv = (dst[:, 0:ncols]
                                .rearrange("p (s n) -> p s n", n=L)
                                [:, :, g * 4:g * 4 + nb]
                                .transpose([0, 2, 1]))
                        ev = reader(eng,
                                    (lambda e, dstv=dstv, src=src:
                                     e.copy(dstv, src)) if eng == "scalar"
                                    else (lambda e, dstv=dstv, src=src:
                                          e.tensor_copy(dstv, src)), bank)
                        if dst is v_pm:
                            v_ev[g] = ev
                        else:
                            qk_ev[(bank, g)] = ev
                qk_all = qk_ev[(3, (L + 3) // 4 - 1)]
                v_all = v_ev[(L + 3) // 4 - 1]
                for sbk in range(L):
                    cols = slice(sbk * 128, (sbk + 1) * 128)
                    par = sbk % 2
                    g = sbk * 128 // 512
                    tbank = 4 + par
                    tr = mm_group(tbank,
                                  [("T", v_pm[:, cols], ident[:, :])],
                                  waits=(v_all, small_ready))
                    vt_w = ((msl_dma,)
                            if (bi == 0 and h == 0 and sbk == 0) else ())
                    vt_ev = reader("vector",
                                   lambda e, cols=cols, bank=tbank:
                                   e.tensor_copy(v_tm[:, cols],
                                                 ps[bank][:, 0:128]), tbank,
                                   waits=vt_w)
                    bank = 6 + par
                    mm_group(bank, [(k_sb[:, cols], q_sb[:, cols],
                                     ps[bank][:, 0:128])],
                             waits=(qk_all,))
                    reader("vector",
                           lambda e, h=h, bank=bank, par=par:
                           e.tensor_add(t_sc[par][:, :], ps[bank][:, 0:128],
                                        mb_sb[h][:, :]), bank,
                           waits=(small_ready, tsc_guard[par]))
                    nrm_ev = b.emit("vector", lambda e, par=par: e.reduce_max(
                        nrm[par][:, 0:1], t_sc[par][:, :], AX.X, negate=True))
                    exp_ev = b.emit("scalar",
                                    lambda e, par=par: e.activation(
                                        e_bf[par][:, :], t_sc[par][:, :],
                                        AF.Exp, bias=nrm[par][:, 0:1],
                                        scale=1.0,
                                        accum_out=rs[par][:, 0:1]),
                                    waits=(nrm_ev, ebf_guard[par]))
                    tsc_guard[par] = exp_ev
                    rec = b.emit("vector", lambda e, par=par: e.reciprocal(
                        rsi[par][:, 0:1], rs[par][:, 0:1]), waits=(exp_ev,))
                    p_ev = b.emit("scalar",
                                  lambda e, par=par, cols=cols:
                                  e.activation(p_bf[par][:, :], v_tm[:, cols],
                                               AF.Copy, bias=0.0,
                                               scale=rsi[par][:, 0:1]),
                                  waits=(rec, vt_ev, pbf_guard[par]))
                    if "blk" in dump and bi == 0 and h == 0 and sbk < 2:
                        bo1 = dbg_out("tsc%d" % sbk, (128, 128))
                        tp1 = b.dma("gpsimd", bo1, t_sc[par][:, :],
                                    waits=(p_ev,), lane="gt")
                        tsc_guard[par] = tp1
                        bo2 = dbg_out("pb%d" % sbk, (128, 128), BF16)
                        tp2 = b.dma("gpsimd", bo2, p_bf[par][:, :],
                                    waits=(p_ev,), lane="gt")
                        bo3 = dbg_out("vt%d" % sbk, (128, 128), BF16)
                        tp3 = b.dma("gpsimd", bo3, v_tm[:, cols],
                                    waits=(vt_ev,), lane="gt")
                        bo4 = dbg_out("eb%d" % sbk, (128, 128), BF16)
                        tp4 = b.dma("gpsimd", bo4, e_bf[par][:, :],
                                    waits=(p_ev,), lane="gt")
                        bo5a = dbg_out("nr%d" % sbk, (128, 1))
                        bo5b = dbg_out("rsum%d" % sbk, (128, 1))
                        bo5c = dbg_out("rsin%d" % sbk, (128, 1))
                        b.dma("gpsimd", bo5a, nrm[par][:, 0:1],
                              waits=(p_ev,), lane="gt")
                        b.dma("gpsimd", bo5b, rs[par][:, 0:1],
                              waits=(p_ev,), lane="gt")
                        tp5 = b.dma("gpsimd", bo5c, rsi[par][:, 0:1],
                              waits=(p_ev,), lane="gt")
                        ebf_guard[par] = tp4
                    avbank = 2 + par
                    av = mm_group(avbank,
                                  [(p_bf[par][:, :], e_bf[par][:, :],
                                    ps[avbank][:, 0:128])],
                                  waits=(p_ev,))
                    pbf_guard[par] = av
                    ebf_guard[par] = av
                    if "blk" in dump and bi == 0 and h == 0 and sbk < 2:
                        pbf_guard[par] = tp2
                    att_ev[h] = reader(
                        "scalar",
                        lambda e, h=h, cols=cols, bank=avbank:
                        e.copy(at_sb[h][:, cols], ps[bank][:, 0:128]), avbank)
            if "att" in dump and bi == 0:
                for h in range(4):
                    ao = dbg_out("att%d" % h, (128, 128 * LS[h]), BF16)
                    b.dma("gpsimd", ao, at_sb[h][:, :], waits=(att_ev[h],), lane="gt")
            if "qkv" in dump and bi == 0:
                qo = dbg_out("qk", (2, 128, 2048), BF16)
                b.dma("gpsimd", qo[0], q_sb[:, :], waits=(att_ev[3],), lane="gt")
                b.dma("gpsimd", qo[1], k_sb[:, :], waits=(att_ev[3],), lane="gt")
                vo = dbg_out("vtm", (128, 2048), BF16)
                b.dma("gpsimd", vo, v_tm[:, :], waits=(att_ev[3],), lane="gt")

            if bi == 0:
                for r in range(8):
                    for dc in range(8):
                        m_ready = b.dma(
                            "gpsimd", m_sb[dc][:, r * 128:(r + 1) * 128],
                            mfl_d[r * D + dc * 128:r * D + (dc + 1) * 128, :],
                            waits=(cc_ev, att_ev[3]), lane="gm")
                if "msb" in dump:
                    mo = dbg_out("msb", (8, 128, 1024), BF16)
                    for dc in range(8):
                        b.dma("gpsimd", mo[dc], m_sb[dc][:, :],
                              waits=(m_ready,), lane="gt")

            last_ffn_mm = None
            for oc in range(8):
                xg_ld = []
                for hf in range(2):
                    xg_ld.append(b.dma(
                        "sync", xg_sb[hf][:, :],
                        xg_d[bi, oc * 128:(oc + 1) * 128,
                             hf * 1024:(hf + 1) * 1024],
                        waits=tuple(xg_guard[hf]), lane=f"xg{hf}"))
                for jt in range(4):
                    ybank = 0 if jt % 2 == 0 else 2
                    sbank = 1 if jt % 2 == 0 else 3
                    gj = jt % 2
                    mms = []
                    for dc in range(8):
                        for (pc, st0, stp, cnt) in JT_FFN[jt]:
                            mms.append((m_sb[dc][:, oc * 128:(oc + 1) * 128],
                                        _sl(xn[dc][:, :], st0, stp, cnt),
                                        ps[ybank][:, pc:pc + cnt]))
                    mm_group(ybank, mms, waits=(xn_ready, m_ready))
                    silu = reader("scalar",
                                  lambda e, gj=gj, oc=oc, bank=ybank:
                                  e.activation(g_sb[gj][:, :], ps[bank][:, :],
                                               AF.Silu,
                                               bias=fb_sb[:, oc:oc + 1],
                                               scale=1.0), ybank,
                                  waits=(small_ready, g_guard[gj]))
                    mms = []
                    opw = [small_ready]
                    for (h, pc, st0, stp, cnt) in JT_OP[jt]:
                        mms.append((wo_sb[h][:, oc * 128:(oc + 1) * 128],
                                    _sl(at_sb[h][:, :], st0, stp, cnt),
                                    ps[sbank][:, pc:pc + cnt]))
                        opw.append(att_ev[h])
                    last_ffn_mm = mm_group(sbank, mms, waits=tuple(opw))
                    comb = reader("vector",
                                  lambda e, gj=gj, oc=oc, bank=sbank:
                                  e.scalar_tensor_tensor(
                                      og_sb[gj][:, :], ps[bank][:, :],
                                      bo_sb[:, oc:oc + 1], g_sb[gj][:, :],
                                      OP.add, OP.mult), sbank,
                                  waits=(silu,))
                    g_guard[gj] = comb
                    hf = jt // 2
                    add = b.emit("vector",
                                 lambda e, gj=gj, jt=jt, hf=hf:
                                 e.tensor_add(
                                     out_sb[hf][:, (jt % 2) * 512:
                                                (jt % 2) * 512 + 512],
                                     og_sb[gj][:, :],
                                     xg_sb[hf][:, (jt % 2) * 512:
                                               (jt % 2) * 512 + 512]),
                                 waits=(comb, xg_ld[hf], og_dma[hf]))
                    if jt % 2 == 1:
                        stv = b.dma("sync",
                                    out_d[bi, oc * 128:(oc + 1) * 128,
                                          hf * 1024:(hf + 1) * 1024],
                                    out_sb[hf][:, :], waits=(add,),
                                    lane=f"st{hf}")
                        og_dma[hf] = stv
                        xg_guard[hf] = [add]
                        last_stores[hf] = stv
            for dc in range(8):
                xn_guard[dc] = [last_ffn_mm]
            lnbuf_guard = [apply_ev[7]]

        b.emit("gpsimd", lambda e: e.engine_nop(),
               waits=tuple(s for s in last_stores if s is not None))

        for op in b.ops:
            if op.is_dma:
                op.needed = True
            for w in op.waits:
                w.needed = True
        ctr = {}
        for op in b.ops:
            if not op.needed:
                continue
            key = ("dma_" + op.lane) if op.is_dma else op.eng
            op.sem = key
            ctr[key] = ctr.get(key, 0) + (16 if op.is_dma else 1)
            op.val = ctr[key]
        sems = {key: stack.enter_context(nc.semaphore("sem_" + key))
                for key in sorted(ctr)}
        with nc.Block() as block:
            def mk(name):
                def run(eng_obj):
                    for op in b.ops:
                        if op.eng != name:
                            continue
                        for w in op.waits:
                            if (not w.is_dma) and w.eng == name:
                                continue
                            eng_obj.wait_ge(sems[w.sem], w.val)
                        for (h, v) in op.raw_waits:
                            eng_obj.wait_ge(h, v)
                        inst = op.fn(eng_obj)
                        if op.needed:
                            inst.then_inc(sems[op.sem],
                                          16 if op.is_dma else 1)
                return run
            block.sync(mk("sync"))
            block.tensor(mk("tensor"))
            block.vector(mk("vector"))
            block.scalar(mk("scalar"))
            block.gpsimd(mk("gpsimd"))
    return nc


def _stage_inputs(L0_attn_x, L0_attn_Wk, L0_attn_Wq, L0_attn_Wv,
                  L0_attn_W_out, L0_attn_b_out, L0_ffn_w_freq_real,
                  L0_ffn_w_freq_imag, L0_ffn_bias):
    perm = _gperm()
    f32 = np.float32
    idx = np.arange(D)
    ang = (2.0 * np.pi / D) * ((idx[:, None] * idx[None, :]) % D)
    Cm = (np.cos(ang) / 32.0).astype(ml_dtypes.bfloat16)
    Sm = (np.sin(ang) / 32.0).astype(ml_dtypes.bfloat16)
    Snm = (-np.sin(ang) / 32.0).astype(ml_dtypes.bfloat16)
    mb = np.zeros((4, 128, 128), f32)
    for h in range(4):
        L = LS[h]
        p = np.arange(128)
        sl_p, n = p // L, p % L
        sl_j, m = p // L, p % L
        allowed = (sl_p[:, None] == sl_j[None, :]) & \
                  (m[None, :] <= n[:, None])
        mb[h] = np.where(allowed, 0.0, NEG).astype(f32)
    wqt = np.ascontiguousarray(np.asarray(L0_attn_Wq, f32).transpose(0, 2, 1))
    wkt = np.ascontiguousarray(np.asarray(L0_attn_Wk, f32).transpose(0, 2, 1))
    wvt = np.ascontiguousarray(np.asarray(L0_attn_Wv, f32).transpose(0, 2, 1))
    wot = np.ascontiguousarray(np.asarray(L0_attn_W_out, f32).T)
    wrt = np.ascontiguousarray(np.asarray(L0_ffn_w_freq_real, f32).T)
    wit = np.ascontiguousarray(np.asarray(L0_ffn_w_freq_imag, f32).T)
    fbT = np.ascontiguousarray(np.asarray(L0_ffn_bias, f32).reshape(8, 128).T)
    boT = np.ascontiguousarray(
        np.asarray(L0_attn_b_out, f32).reshape(8, 128).T)
    shared = dict(wqt=wqt, wkt=wkt, wvt=wvt, wot=wot, bo_t=boT, fb_t=fbT,
                  wrt=wrt, wit=wit, cdft=Cm, sdft=Sm, sndft=Snm, mb=mb,
                  ident=np.eye(128, dtype=f32))
    xs = np.asarray(L0_attn_x, f32)
    in_maps = []
    for c in range(NCORES):
        xc = xs[BPC * c:BPC * (c + 1)]
        m = dict(shared)
        m["x"] = np.ascontiguousarray(xc.transpose(0, 2, 1))
        m["xg"] = np.ascontiguousarray(xc[:, perm, :].transpose(0, 2, 1))
        sl = slice(128 * c, 128 * (c + 1))
        m["csl"] = np.ascontiguousarray(Cm[:, sl])
        m["ssl"] = np.ascontiguousarray(Sm[:, sl])
        m["snsl"] = np.ascontiguousarray(Snm[:, sl])
        in_maps.append(m)
    return in_maps, perm


def kernel(L0_attn_x, L0_attn_Wk, L0_attn_Wq, L0_attn_Wv, L0_attn_W_out,
           L0_attn_b_out, L0_ffn_w_freq_real, L0_ffn_w_freq_imag,
           L0_ffn_bias):
    in_maps, perm = _stage_inputs(
        L0_attn_x, L0_attn_Wk, L0_attn_Wq, L0_attn_Wv, L0_attn_W_out,
        L0_attn_b_out, L0_ffn_w_freq_real, L0_ffn_w_freq_imag, L0_ffn_bias)
    nc = build(dump=tuple(DEBUG_TAPS))
    res = run_bass_kernel_spmd(nc, in_maps, core_ids=list(range(NCORES)))
    out = np.empty((B_FULL, T, D), np.float32)
    for c in range(NCORES):
        o = np.asarray(res.results[c]["out"])
        out[BPC * c:BPC * (c + 1)][:, perm, :] = o.transpose(0, 2, 1)
    if DEBUG_TAPS:
        kernel.debug = [{k: np.asarray(v) for k, v in r.items()
                         if k.startswith("dbg_")} for r in res.results]
    return out
